# version 19
# baseline (speedup 1.0000x reference)
import numpy as np
import ml_dtypes
import concourse.bass as bass
import concourse.mybir as mybir
from concourse.bass_utils import run_bass_kernel_spmd

F32 = mybir.dt.float32
BF16 = mybir.dt.bfloat16
AF = mybir.ActivationFunctionType
ALU = mybir.AluOpType
AX = mybir.AxisListType

GAMMA = 0.96875
DM = 2048
HALF = DM // 2
HD = 5632
SB = 512
C = 512
SEG = 2048
NCHUNK = SEG // C
NPRIOR = 2
B, S = 2, 8192
GC = float(GAMMA ** C)
QEPS = 1e-8
LNEPS = 1e-5

PERM = np.concatenate([np.arange(0, DM, 2), np.arange(1, DM, 2)])


def host_tables(seg_idx, n_prior, n_own):
    j = np.arange(HALF, dtype=np.float64)
    scale_vec = (2 * j + 0.4 * DM) / (1.4 * DM)
    inv_freq = 1.0 / 10000.0 ** (j / HALF)

    own0 = seg_idx * SEG
    own = np.arange(own0, own0 + n_own * C, dtype=np.float64)
    ctx0 = own0 - n_prior * C
    ctx = np.arange(ctx0, ctx0 + (n_prior + n_own) * C, dtype=np.float64)

    def mk(pos, up):
        a = np.mod(pos, C)
        ang = pos[None, :] * inv_freq[:, None]
        sc = scale_vec[:, None] ** (pos[None, :] / SB)
        if up:
            g = GAMMA ** (a + 1.0)
            s = sc
        else:
            g = GAMMA ** (-(a + 1.0))
            s = 1.0 / sc
        return ((np.cos(ang) * s * g[None, :]).astype(np.float32),
                (np.sin(ang) * s * g[None, :]).astype(np.float32))

    cq, sq = mk(own, True)
    ck, sk = mk(np.maximum(ctx, 0.0), False)
    return cq, sq, ck, sk


def host_masks():
    m = np.zeros((4 * 128, C), dtype=np.float32)
    for mc in range(4):
        ml = np.arange(128)[:, None] + mc * 128
        n = np.arange(C)[None, :]
        m[mc * 128:(mc + 1) * 128] = (n >= ml).astype(np.float32)
    return m.astype(ml_dtypes.bfloat16)


class Sched:
    ENGS = ("sync", "gpsimd", "tensor", "vector", "scalar")

    def __init__(self, nc):
        self.nc = nc
        self.ops = {e: [] for e in self.ENGS}
        self.sems = {}
        self.cnt = {}

    def sem(self, name):
        if name not in self.sems:
            self.sems[name] = self.nc.alloc_semaphore(name)
            self.cnt[name] = 0
        return self.sems[name]

    def emit(self, eng, fn, waits=(), inc=None):
        for (s, v) in waits:
            assert s in self.cnt and v <= self.cnt[s], (s, v, self.cnt.get(s))
        if inc:
            self.cnt[inc[0]] += inc[1]
        self.ops[eng].append((fn, tuple(waits), inc))

    def c(self, name):
        return self.cnt.get(name, 0)

    def bump(self, name, n):
        self.cnt[name] += n

    def wall(self, *names):
        return [(n, self.cnt[n]) for n in names if self.cnt.get(n, 0) > 0]

    def build_block(self):
        nc = self.nc
        with nc.Block() as blk:
            for eng in self.ENGS:
                ops = list(self.ops[eng])

                def mk(ops):
                    def run(e):
                        for fn, waits, inc in ops:
                            for (s, v) in waits:
                                e.wait_ge(self.sems[s], v)
                            last = fn(e)
                            if inc is not None:
                                assert last is not None
                                last.then_inc(self.sems[inc[0]], inc[1])
                    return run
                getattr(blk, eng)(mk(ops))


def build_program(do_ffn=True, n_prior=NPRIOR, n_own=NCHUNK):
    nc = bass.Bass("TRN2", target_bir_lowering=False, debug=False)
    TOK_CTX = (n_prior + n_own) * C

    Xc = nc.dram_tensor("Xc", [TOK_CTX, DM], F32, kind="ExternalInput").ap()
    Wq = nc.dram_tensor("Wq", [DM, DM], F32, kind="ExternalInput").ap()
    Wk = nc.dram_tensor("Wk", [DM, DM], F32, kind="ExternalInput").ap()
    Wv = nc.dram_tensor("Wv", [DM, DM], F32, kind="ExternalInput").ap()
    TCQ = nc.dram_tensor("TCQ", [HALF, n_own * C], F32, kind="ExternalInput").ap()
    TSQ = nc.dram_tensor("TSQ", [HALF, n_own * C], F32, kind="ExternalInput").ap()
    TCK = nc.dram_tensor("TCK", [HALF, TOK_CTX], F32, kind="ExternalInput").ap()
    TSK = nc.dram_tensor("TSK", [HALF, TOK_CTX], F32, kind="ExternalInput").ap()
    MSK = nc.dram_tensor("MSK", [4 * 128, C], BF16, kind="ExternalInput").ap()
    IDN = nc.dram_tensor("IDN", [128, 128], BF16, kind="ExternalInput").ap()
    IDG = nc.dram_tensor("IDG", [128, 128], F32, kind="ExternalInput").ap()
    Y = nc.dram_tensor("Y", [n_own * C, DM], F32, kind="ExternalOutput").ap()
    WqB = nc.dram_tensor("WqB", [DM, DM], BF16, kind="Internal").ap()
    WkB = nc.dram_tensor("WkB", [DM, DM], BF16, kind="Internal").ap()
    WvB = nc.dram_tensor("WvB", [DM, DM], BF16, kind="Internal").ap()
    TCQb = nc.dram_tensor("TCQb", [HALF, n_own * C], BF16, kind="Internal").ap()
    TSQb = nc.dram_tensor("TSQb", [HALF, n_own * C], BF16, kind="Internal").ap()
    TCKb = nc.dram_tensor("TCKb", [HALF, TOK_CTX], BF16, kind="Internal").ap()
    TSKb = nc.dram_tensor("TSKb", [HALF, TOK_CTX], BF16, kind="Internal").ap()
    if do_ffn:
        w1d = nc.dram_tensor("w1", [HD, DM], F32, kind="ExternalInput").ap()
        w2d = nc.dram_tensor("w2", [HD, DM], F32, kind="ExternalInput").ap()
        w3d = nc.dram_tensor("w3", [DM, HD], F32, kind="ExternalInput").ap()
        w12qT = nc.dram_tensor("w12qT", [DM, 2 * HD], BF16, kind="Internal").ap()
        w3qT = nc.dram_tensor("w3qT", [HD, DM], BF16, kind="Internal").ap()
        gtmp = nc.dram_tensor("gtmp", [3, 128], F32, kind="Internal").ap()
        gthr = nc.dram_tensor("gthr", [1, 4], F32, kind="Internal").ap()

    ARENA = 210176
    nc.alloc_sbuf_tensor("arena", [128, ARENA], mybir.dt.uint8)
    ABASE = nc.lookup_mloc("arena").addr
    off = [0]

    def at(name, shape, dtype, o=None):
        nbytes = (shape[1] * (4 if dtype == F32 else 2) + 31) // 32 * 32
        if o is None:
            o = off[0]
            off[0] += nbytes
        assert o + nbytes <= ARENA and o % 32 == 0, (name, o, nbytes)
        return nc.alloc_sbuf_tensor_at(name, shape, dtype,
                                       offset=ABASE + o).ap()

    Rst = at("Rst", [128, 16 * DM], BF16)
    idn = at("idn", [128, 128], BF16)
    idg = at("idg", [128, 128], BF16)
    stats = at("stats", [128, 32], F32)
    thr = at("thr", [128, 8], F32)
    masks = at("masks", [128, 4 * C], BF16)
    stg = [at(f"stg{i}", [128, 4 * 512], BF16) for i in range(2)]
    wsl2 = [at(f"wsl2{i}", [128, 16 * 128], BF16) for i in range(2)]
    xnf = at("xnf", [128, 16 * C], BF16)
    AB = off[0]
    tabq = [at(f"tabq{i}", [128, 2 * C], BF16) for i in range(2)]
    tabk = [at(f"tabk{i}", [128, 2 * C], BF16) for i in range(2)]
    Qd = at("Qd", [128, 16 * C], BF16)
    Kd = at("Kd", [128, 16 * C], BF16)
    Ktm = at("Ktm", [128, 4 * C], BF16)
    Vtm = at("Vtm", [128, 4 * DM], BF16)
    att = at("att", [128, 4 * C], BF16)
    wsl = [at(f"wsl{i}", [128, 16 * 128], BF16) for i in range(2)] + wsl2
    wvs = [at(f"wvs{i}", [128, 8 * C], BF16) for i in range(2)]
    RT0 = off[0]
    rtmp0 = at("rtmp0", [128, C], F32)
    rtmp1 = at("rtmp1", [128, C], F32)
    sqdump = at("sqdump", [128, DM], BF16, RT0)
    xsl = at("xsl", [128, DM], BF16)
    XA = off[0]
    xa = at("xa", [128, DM], BF16)
    xb = at("xb", [128, DM], BF16)
    AE = off[0]
    assert AE <= ARENA, AE
    if do_ffn:
        o = AB
        hid = at("hid", [128, 44 * C], BF16, o); o += 44 * C * 2
        w12 = []
        for i in range(2):
            w12.append(at(f"w12{i}", [128, 2 * 16 * 128], BF16, o))
            o += 2 * 16 * 128 * 2
        w3s = []
        for i in range(2):
            w3s.append(at(f"w3s{i}", [128, 8 * C], BF16, o)); o += 8 * C * 2
        sgt = at("sgt", [128, C], F32, o); o += C * 4
        assert o <= AE, (o, AE)
        QB = AB + 4 * (2 * C * 2)
        qtile = at("qtile", [128, DM], F32, QB)
        qtmp = at("qtmp", [128, DM], F32, QB + DM * 4)
        attoff = QB + 16 * C * 2 * 2 + 4 * C * 2 + 4 * DM * 2
        qout = at("qout", [128, DM], BF16, attoff)
        trv = at("trv", [128, 2048], BF16, AB)

    P = lambda n, s, d: nc.alloc_psum_tensor(n, s, d).ap()
    pb = [P(f"pb{i}", [128, 512], F32) for i in range(6)]
    tb = [P(f"tb{i}", [128, 1024], BF16) for i in range(2)]

    sc = Sched(nc)
    for s in ("sX", "sTab0", "sTab1", "sW0", "sW1", "sWv0", "sWv1", "sOut",
              "sT", "sV", "sS", "sG", "sQw0", "sQw1", "sQl0", "sQl1", "sIni",
              "sF0", "sF1", "sPC", "sQv0", "sQv1", "sQg", "sYP", "sPCv", "sPCq", "sOut1", "sTg0", "sTg1", "sIdg", "sTcv"):
        sc.sem(s)

    xnf3 = xnf.rearrange("p (c j) -> p c j", c=16)
    stg3 = [s.rearrange("p (c j) -> p c j", c=4) for s in stg]
    Qd3 = Qd.rearrange("p (c j) -> p c j", c=16)
    Kd3 = Kd.rearrange("p (c j) -> p c j", c=16)
    Ktm3 = Ktm.rearrange("p (c j) -> p c j", c=4)
    Vtm3 = Vtm.rearrange("p (c j) -> p c j", c=4)
    att3 = att.rearrange("p (c j) -> p c j", c=4)
    Rst3 = Rst.rearrange("p (c j) -> p c j", c=16)
    wsl3 = [w.rearrange("p (c j) -> p c j", c=16) for w in wsl]
    wvs3 = [w.rearrange("p (c j) -> p c j", c=8) for w in wvs]
    msk3 = masks.rearrange("p (c j) -> p c j", c=4)
    if do_ffn:
        hid3 = hid.rearrange("p (c j) -> p c j", c=44)
        w3sv = [w.rearrange("p (c j) -> p c j", c=8) for w in w3s]

    def setup():
        def f(e):
            for mc in range(4):
                e.dma_start(msk3[:, mc, :],
                            MSK[mc * 128:(mc + 1) * 128, :]).then_inc(
                                sc.sems["sIni"], 16)
            e.dma_start(idn, IDN).then_inc(sc.sems["sIni"], 16)
            return None
        sc.bump("sIni", 80)
        sc.emit("sync", f)
        def fidg(e):
            e.dma_start(idg, IDG).then_inc(sc.sems["sIdg"], 16)
            return None
        sc.bump("sIdg", 16)
        sc.emit("gpsimd", fidg)
        for eng in ("tensor", "vector", "scalar"):
            sc.emit(eng, lambda e: None, waits=[("sIni", 80), ("sIdg", 16)])
        sc.emit("vector", lambda e: e.memset(Rst, 0.0), inc=("sV", 1))

    lnx = [0]

    def ln_chunk(ch):
        slabs = [xa, xb]
        def load(tc):
            def fxl(e, tc=tc):
                return e.dma_start(slabs[tc % 2],
                                   Xc[ch * C + tc * 128:ch * C + (tc + 1) * 128, :])
            lnx[0] += 1
            w = sc.wall("sV") if tc >= 2 else sc.wall("sV", "sS")
            sc.emit("gpsimd", fxl, waits=w, inc=("sX", 16))
        load(0)
        load(1)
        for tc in range(4):
            sl = slabs[tc % 2]
            def fsq(e, tc=tc, sl=sl):
                e.drain()
                return e.activation(sqdump, sl, AF.Square,
                                    accum_out=stats[:, 20 + tc:21 + tc])
            sc.emit("scalar", fsq,
                    waits=sc.wall("sV") + [("sX", sc.c("sX"))],
                    inc=("sS", 1))
            sb = 5 * tc
            def fstat(e, tc=tc, sb=sb, sl=sl):
                e.drain()
                e.tensor_reduce(stats[:, sb:sb + 1], sl, AX.X, ALU.add)
                e.drain()
                e.tensor_scalar(stats[:, sb + 1:sb + 2], stats[:, sb:sb + 1],
                                1.0 / DM, None, ALU.mult)
                e.tensor_scalar(stats[:, sb + 2:sb + 3],
                                stats[:, 20 + tc:21 + tc], 1.0 / DM, None,
                                ALU.mult)
                e.drain()
                e.tensor_mul(stats[:, sb + 3:sb + 4], stats[:, sb + 1:sb + 2],
                             stats[:, sb + 1:sb + 2])
                e.drain()
                e.tensor_sub(stats[:, sb + 3:sb + 4], stats[:, sb + 2:sb + 3],
                             stats[:, sb + 3:sb + 4])
                e.drain()
                return e.tensor_scalar(stats[:, sb + 3:sb + 4],
                                       stats[:, sb + 3:sb + 4], LNEPS, None,
                                       ALU.add)
            sc.emit("vector", fstat, waits=sc.wall("sS"), inc=("sV", 1))
            def fsqrt(e, sb=sb):
                e.drain()
                return e.activation(stats[:, sb + 4:sb + 5],
                                    stats[:, sb + 3:sb + 4], AF.Sqrt)
            sc.emit("scalar", fsqrt, waits=sc.wall("sV"), inc=("sS", 1))
            def fnorm(e, tc=tc, sb=sb, sl=sl):
                e.drain()
                e.reciprocal(stats[:, 25:26], stats[:, sb + 4:sb + 5])
                e.drain()
                e.tensor_mul(stats[:, 26:27], stats[:, 25:26], stats[:, 25:26])
                e.drain()
                e.tensor_mul(stats[:, 26:27], stats[:, 26:27],
                             stats[:, sb + 3:sb + 4])
                e.drain()
                e.tensor_scalar(stats[:, 26:27], stats[:, 26:27], -0.5, 1.5,
                                ALU.mult, ALU.add)
                e.drain()
                e.tensor_mul(stats[:, 25:26], stats[:, 25:26], stats[:, 26:27])
                e.drain()
                e.tensor_mul(stats[:, 27:28], stats[:, sb + 1:sb + 2],
                             stats[:, 25:26])
                e.drain()
                return e.tensor_scalar(xsl, sl, stats[:, 25:26],
                                       stats[:, 27:28], ALU.mult, ALU.subtract)
            sc.emit("vector", fnorm, waits=sc.wall("sS", "sT"), inc=("sV", 1))
            if tc + 2 < 4:
                load(tc + 2)
            for half in range(2):
                def ftr(e, half=half):
                    last = None
                    for i in range(8):
                        dc = half * 8 + i
                        last = e.transpose(tb[half][:, i * 128:(i + 1) * 128],
                                           xsl[:, dc * 128:(dc + 1) * 128], idn)
                    return last
                sc.emit("tensor", ftr,
                        waits=sc.wall("sV", "sS", "sQw0", "sQw1"),
                        inc=("sT", 1))
                def fev(e, tc=tc, half=half):
                    src = tb[half].rearrange("p (c j) -> p c j", c=8)
                    dst = xnf3[:, half * 8:half * 8 + 8, tc * 128:tc * 128 + 128]
                    return e.copy(dst, src)
                sc.emit("scalar", fev, waits=sc.wall("sT"), inc=("sS", 1))

    def proj_rot(dst3, WB, TC, TS, tabs, col0, pcw, tv):
        def loadp(p):
            b0, b1 = 2 * (p % 2), 2 * (p % 2) + 1
            def fwa(e, p=p, b0=b0):
                e.dma_start(wsl3[b0], WB[:, p * 128:(p + 1) * 128]
                            .rearrange("(c p) j -> p c j", p=128)).then_inc(
                                sc.sems["sW0"], 16)
                return None
            sc.bump("sW0", 16)
            sc.emit("sync", fwa, waits=sc.wall("sT") + [(pcw, 16)])
            def fwb(e, p=p, b1=b1):
                e.dma_start(wsl3[b1], WB[:, (p + 8) * 128:(p + 9) * 128]
                            .rearrange("(c p) j -> p c j", p=128)).then_inc(
                                sc.sems["sW1"], 16)
                return None
            sc.bump("sW1", 16)
            sc.emit("sync", fwb, waits=sc.wall("sT") + [(pcw, 16)])
        loadp(0)
        loadp(1)
        pend = None
        for p in range(8):
            par = p % 2
            b0, b1 = (0, 1) if par == 0 else (2, 3)
            st = "sTg0" if par == 0 else "sTg1"
            def ft(e, p=p, st=st, tab=tabs[par]):
                e.dma_start(tab[:, 0:C],
                            TC[p * 128:(p + 1) * 128, col0:col0 + C]).then_inc(
                                sc.sems[st], 16)
                e.dma_start(tab[:, C:2 * C],
                            TS[p * 128:(p + 1) * 128, col0:col0 + C]).then_inc(
                                sc.sems[st], 16)
                return None
            sc.bump(st, 32)
            sc.emit("gpsimd", ft,
                    waits=sc.wall("sV", "sQw0", "sQw1") + [("sTcv", tv)])
            for h, bk in ((0, b0), (1, b1)):
                sw = "sW0" if h == 0 else "sW1"
                wb = 2 * par + h
                def fm(e, wb=wb, bk=bk):
                    last = None
                    for k in range(16):
                        last = e.matmul(pb[bk], wsl3[wb][:, k, :],
                                        xnf3[:, k, :],
                                        start=(k == 0), stop=(k == 15))
                    return last
                sc.emit("tensor", fm, waits=sc.wall(sw, "sS", "sV"),
                        inc=("sT", 1))
            if p + 2 < 8:
                loadp(p + 2)
            if pend is not None:
                sc.emit("vector", pend[0], waits=pend[1], inc=("sV", 1))
            def fr(e, p=p, tab=tabs[par], b0=b0, b1=b1):
                e.drain()
                cT = tab[:, 0:C]
                sT = tab[:, C:2 * C]
                e.tensor_mul(rtmp0, pb[b0], cT)
                e.tensor_mul(rtmp1, pb[b1], sT)
                e.drain()
                e.tensor_sub(dst3[:, p, :], rtmp0, rtmp1)
                e.drain()
                e.tensor_mul(rtmp0, pb[b1], cT)
                e.tensor_mul(rtmp1, pb[b0], sT)
                e.drain()
                return e.tensor_add(dst3[:, p + 8, :], rtmp0, rtmp1)
            pend = (fr, sc.wall("sT", "sS", st))
        sc.emit("vector", pend[0], waits=pend[1], inc=("sV", 1))

    def proj_v():
        seqs = [(vs, dh) for vs in range(4) for dh in range(2)]
        def mkload(i):
            vs, dh = seqs[i]
            buf = i % 2
            swn = "sWv0" if buf == 0 else "sWv1"
            def fw(e, vs=vs, dh=dh, buf=buf, swn=swn):
                src = WvB[dh * 1024:(dh + 1) * 1024, vs * 512:(vs + 1) * 512]
                e.dma_start(wvs3[buf], src.rearrange("(c p) j -> p c j",
                                                     p=128)).then_inc(
                                                         sc.sems[swn], 16)
                return None
            sc.bump(swn, 16)
            sc.emit("sync", fw, waits=sc.wall("sT") + [("sPCv", 16)])
        mkload(0)
        mkload(1)
        for i, (vs, dh) in enumerate(seqs):
            buf = i % 2
            swn = "sWv0" if buf == 0 else "sWv1"
            def fm(e, vs=vs, dh=dh, buf=buf):
                last = None
                for tc in range(4):
                    for k in range(8):
                        dc = dh * 8 + k
                        last = e.matmul(pb[tc],
                                        xnf3[:, dc, tc * 128:(tc + 1) * 128],
                                        wvs3[buf][:, k, :],
                                        start=(dc == 0), stop=(dc == 15))
                return last
            sc.emit("tensor", fm, waits=sc.wall(swn, "sS", "sV"), inc=("sT", 1))
            if i + 2 < len(seqs):
                mkload(i + 2)
            if dh == 1:
                for tc in range(4):
                    def fe(e, vs=vs, tc=tc):
                        return e.copy(Vtm3[:, tc, vs * 512:(vs + 1) * 512],
                                      pb[tc])
                    sc.emit("scalar", fe, waits=sc.wall("sT"), inc=("sS", 1))

    def scores():
        for mc in range(4):
            def fm(e, mc=mc):
                last = None
                for k in range(16):
                    last = e.matmul(pb[mc], Kd3[:, k, mc * 128:(mc + 1) * 128],
                                    Qd3[:, k, :], start=(k == 0), stop=(k == 15))
                return last
            sc.emit("tensor", fm, waits=sc.wall("sV", "sS"), inc=("sT", 1))
        for mc in range(4):
            def fk(e, mc=mc):
                return e.tensor_mul(att3[:, mc, :], pb[mc], msk3[:, mc, :])
            sc.emit("vector", fk, waits=sc.wall("sT"), inc=("sV", 1))

    stg_free = [0, 0]

    def av_cross(oc):
        pend = []
        tiles = [(vs, ncb) for vs in range(4) for ncb in range(4)]
        for i, (vs, ncb) in enumerate(tiles):
            bank = pb[i % 6]
            def fm(e, ncb=ncb, vs=vs, bank=bank):
                first = True
                last = None
                for mc in range(ncb + 1):
                    last = e.matmul(bank, att3[:, mc, ncb * 128:(ncb + 1) * 128],
                                    Vtm3[:, mc, vs * 512:(vs + 1) * 512],
                                    start=first, stop=False)
                    first = False
                for k in range(16):
                    last = e.matmul(bank, Qd3[:, k, ncb * 128:(ncb + 1) * 128],
                                    Rst3[:, k, vs * 512:(vs + 1) * 512],
                                    start=False, stop=(k == 15))
                return last
            sc.emit("tensor", fm, waits=sc.wall("sV", "sS"), inc=("sT", 1))
            sb = vs % 2
            def fe(e, ncb=ncb, vs=vs, bank=bank, sb=sb):
                return e.tensor_copy(stg3[sb][:, ncb, :], bank)
            so = "sOut" if sb == 0 else "sOut1"
            pend.append((fe, sc.wall("sT") + [(so, stg_free[sb]),
                                              ("sYP", 16)]))
            if len(pend) > 5:
                f, w = pend.pop(0)
                sc.emit("vector", f, waits=w, inc=("sV", 1))
            if ncb == 3:
                while pend:
                    f, w = pend.pop(0)
                    sc.emit("vector", f, waits=w, inc=("sV", 1))
                def fst(e, vs=vs, oc=oc, sb=sb):
                    dst = Y[oc * C:(oc + 1) * C, vs * 512:(vs + 1) * 512]
                    return e.dma_start(dst.rearrange("(c p) j -> p c j", p=128),
                                       stg3[sb], accum_op=ALU.add)
                so = "sOut" if sb == 0 else "sOut1"
                sc.emit("gpsimd", fst, waits=sc.wall("sV"), inc=(so, 16))
                stg_free[sb] = sc.c(so)

    def state_update(last=False):
        if last:
            return
        pend = []
        for js in range(4):
            for g in range(2):
                def ftr(e, js=js, g=g):
                    last = None
                    for i in range(2):
                        mc = g * 2 + i
                        for jj in range(4):
                            jc = js * 4 + jj
                            last = e.transpose(
                                tb[g][:, (i * 4 + jj) * 128:(i * 4 + jj + 1) * 128],
                                Kd3[:, jc, mc * 128:(mc + 1) * 128], idn)
                    return last
                sc.emit("tensor", ftr,
                        waits=sc.wall("sV", "sS", "sQw0", "sQw1"),
                        inc=("sT", 1))
                def fev(e, g=g):
                    src = tb[g].rearrange("p (c j) -> p c j", c=2)
                    dst = Ktm3[:, g * 2:g * 2 + 2, :]
                    return e.tensor_scalar(dst, src, GC, None, ALU.mult)
                sc.emit("vector", fev, waits=sc.wall("sT"), inc=("sV", 1))
            for jj in range(4):
                jc = js * 4 + jj
                for vs in range(4):
                    i = js * 16 + jj * 4 + vs
                    bank = pb[i % 6]
                    def fm(e, jc=jc, jj=jj, vs=vs, bank=bank):
                        for mc in range(4):
                            e.matmul(bank, Ktm3[:, mc, jj * 128:(jj + 1) * 128],
                                     Vtm3[:, mc, vs * 512:(vs + 1) * 512],
                                     start=(mc == 0), stop=False)
                        return e.matmul(bank, idg,
                                        Rst3[:, jc, vs * 512:(vs + 1) * 512],
                                        start=False, stop=True)
                    sc.emit("tensor", fm, waits=sc.wall("sV", "sS"),
                            inc=("sT", 1))
                    def fe(e, jc=jc, vs=vs, bank=bank):
                        return e.tensor_copy(Rst3[:, jc, vs * 512:(vs + 1) * 512],
                                             bank)
                    pend.append((fe, sc.wall("sT")))
                    if len(pend) > 5:
                        f, w = pend.pop(0)
                        sc.emit("vector", f, waits=w, inc=("sV", 1))
        for f, w in pend:
            sc.emit("vector", f, waits=w, inc=("sV", 1))

    def quant_units():
        specs = [(w1d, w12qT, HD, 0), (w2d, w12qT, HD, 1), (w3d, w3qT, DM, 2)]
        qv = [0, 0]
        ql = [0, 0]
        it = 0
        for (wd, _, nrow, wi) in specs:
            nt = nrow // 128
            ncols = DM if wi < 2 else HD
            cbs = [(c0, min(DM, ncols - c0)) for c0 in range(0, ncols, DM)]
            for t in range(nt):
                for (c0, cw) in cbs:
                    par = it % 2
                    it += 1
                    first = (t == 0 and c0 == 0)
                    slq = "sQl0" if par == 0 else "sQl1"
                    sqv = "sQv0" if par == 0 else "sQv1"
                    u = ql[par]
                    wv = [(sqv, u)] if u > 0 else []
                    def fl(e, wd=wd, t=t, par=par, c0=c0, cw=cw):
                        buf = qtile if par == 0 else qtmp
                        return e.dma_start(buf[:, 0:cw],
                                           wd[t * 128:(t + 1) * 128,
                                              c0:c0 + cw])
                    ql[par] += 1
                    sc.emit("sync", fl, waits=wv, inc=(slq, 16))
                    def fr(e, first=first, par=par, wi=wi, cw=cw):
                        srcb = qtile if par == 0 else qtmp
                        e.drain()
                        e.tensor_reduce(stats[:, 31:32], srcb[:, 0:cw], AX.X,
                                        ALU.add, apply_absolute_value=True)
                        e.drain()
                        if first:
                            return e.tensor_copy(stats[:, 28 + wi:29 + wi],
                                                 stats[:, 31:32])
                        return e.tensor_add(stats[:, 28 + wi:29 + wi],
                                            stats[:, 28 + wi:29 + wi],
                                            stats[:, 31:32])
                    qv[par] += 1
                    sc.emit("vector", fr, waits=[(slq, ql[par] * 16)],
                            inc=(sqv, 1))
                    if it % 8 == 0:
                        yield
        def fd(e):
            for wi in range(3):
                e.dma_start(gtmp[wi, :],
                            stats[:, 28 + wi:29 + wi]).then_inc(
                                sc.sems["sG"], 16)
            return None
        sc.bump("sG", 48)
        sc.emit("sync", fd, waits=[("sQv0", qv[0]), ("sQv1", qv[1])])
        def fl2(e):
            return e.dma_start(qtmp[0:3, 0:128], gtmp)
        sc.emit("sync", fl2, waits=[("sG", sc.c("sG"))], inc=("sG", 16))
        def fred(e):
            e.drain()
            e.tensor_reduce(stats[0:3, 24:25], qtmp[0:3, 0:128], AX.X, ALU.add)
            e.drain()
            return e.tensor_scalar(stats[0:3, 24:25], stats[0:3, 24:25],
                                   0.5 / (HD * DM), 0.5 * QEPS, ALU.mult,
                                   ALU.add)
        qv[0] += 1
        sc.emit("vector", fred, waits=[("sG", sc.c("sG"))], inc=("sQv0", 1))
        def fst(e):
            for wi in range(3):
                e.dma_start(gthr[0:1, wi:wi + 1],
                            stats[wi:wi + 1, 24:25]).then_inc(sc.sems["sG"], 16)
            return None
        sc.bump("sG", 48)
        sc.emit("sync", fst, waits=[("sQv0", qv[0])])
        def fbc(e):
            return e.dma_start(thr[:, 0:3], gthr[0:1, 0:3].broadcast_to((128, 3)))
        sc.emit("sync", fbc, waits=[("sG", sc.c("sG"))], inc=("sG", 16))
        def fneg(e):
            e.drain()
            return e.tensor_scalar(thr[:, 4:7], thr[:, 0:3], -1.0, None,
                                   ALU.mult)
        qv[0] += 1
        sc.emit("vector", fneg, waits=[("sG", sc.c("sG"))], inc=("sQv0", 1))
        thrdone = [("sQv0", qv[0])]
        qg = [0]
        yield
        last_ftr_sT = 0
        fq_cnt = {0: 0, 1: 0}
        u = 0
        for (wd, wqT, nrow, wi) in specs:
            nt = nrow // 128
            ncols = DM if wi < 2 else HD
            cbs = [(c0, min(DM, ncols - c0)) for c0 in range(0, ncols, DM)]
            for t in range(nt):
                for (c0, cw) in cbs:
                    nb = cw // 128
                    par = u % 2
                    u += 1
                    slq = "sQl0" if par == 0 else "sQl1"
                    wv = ([("sQv0", fq_cnt[par])] if fq_cnt[par] else
                          [("sQv0", qv[0]), ("sQv1", qv[1])])
                    def fl(e, wd=wd, t=t, c0=c0, cw=cw, par=par):
                        buf = qtile if par == 0 else qtmp
                        return e.dma_start(buf[:, 0:cw],
                                           wd[t * 128:(t + 1) * 128,
                                              c0:c0 + cw])
                    sc.emit("sync", fl, waits=wv, inc=(slq, 16))
                    ql[par] += 1
                    def fqv(e, wi=wi, cw=cw, par=par):
                        buf = qtile if par == 0 else qtmp
                        e.drain()
                        e.tensor_single_scalar(qout[:, 0:cw], buf[:, 0:cw],
                                               thr[:, wi:wi + 1], ALU.is_gt)
                        e.drain()
                        e.scalar_tensor_tensor(qout[:, 0:cw], buf[:, 0:cw],
                                               thr[:, 4 + wi:5 + wi],
                                               qout[:, 0:cw],
                                               ALU.is_ge, ALU.add)
                        e.drain()
                        return e.tensor_scalar(qout[:, 0:cw], qout[:, 0:cw],
                                               1.0, None, ALU.subtract)
                    qv[0] += 1
                    fq_cnt[par] = qv[0]
                    sc.emit("vector", fqv,
                            waits=[(slq, ql[par] * 16),
                                   ("sT", min(last_ftr_sT, sc.c("sT")))]
                            + thrdone, inc=("sQv0", 1))
                    for bb in range(0, nb, 8):
                        g = (bb // 8) % 2
                        sq_ = "sQw0" if g == 0 else "sQw1"
                        n8 = min(8, nb - bb)
                        def ftr(e, bb=bb, g=g, n8=n8):
                            last = None
                            for i in range(n8):
                                last = e.transpose(
                                    tb[g][:, i * 128:(i + 1) * 128],
                                    qout[:, (bb + i) * 128:(bb + i + 1) * 128],
                                    idn)
                            return last
                        sc.emit("tensor", ftr,
                                waits=[("sQv0", qv[0])] + sc.wall("sS", sq_),
                                inc=("sT", 1))
                        my_sT = sc.c("sT")
                        def fstw(e, t=t, c0=c0, bb=bb, g=g, n8=n8,
                                 wqT=wqT, wi=wi):
                            tv = trv[:, g * 1024:g * 1024 + n8 * 128]
                            e.copy(tv, tb[g][:, 0:n8 * 128])
                            e.drain()
                            src = tv.rearrange("p (c j) -> p c j", c=n8)
                            r0 = c0 + bb * 128
                            if wi < 2:
                                cc = (2 * t + wi) * 128
                            else:
                                cc = t * 128
                            dst = wqT[r0:r0 + n8 * 128, cc:cc + 128]
                            dst = dst.rearrange("(c p) j -> p c j", p=128)
                            return e.dma_start(dst, src)
                        sc.emit("scalar", fstw,
                                waits=[("sT", my_sT)] + sc.wall(sq_),
                                inc=(sq_, 16))
                    last_ftr_sT = sc.c("sT")
                    if t % 4 == 3:
                        yield

    def ffn_chunk(oc):
        NH = HD // 128
        def loadA(hs):
            db = hs % 2
            sw = "sF0" if db == 0 else "sF1"
            def fwA(e, hs=hs, db=db, sw=sw):
                e.dma_start(w12[db].rearrange("p (c j) -> p c j", c=16),
                            w12qT[:, hs * 256:(hs + 1) * 256]
                            .rearrange("(c p) j -> p c j", p=128)).then_inc(
                                sc.sems[sw], 16)
                return None
            sc.bump(sw, 16)
            sc.emit("sync", fwA, waits=sc.wall("sT", "sQw0", "sQw1"))
        loadA(0)
        loadA(1)
        pend = []
        for hs in range(NH):
            db = hs % 2
            sw = "sF0" if db == 0 else "sF1"
            ba = 2 * (hs % 3)
            def fm(e, db=db, ba=ba):
                wv2 = w12[db].rearrange("p (c j) -> p c j", c=16)
                last = None
                for w in range(2):
                    for k in range(16):
                        last = e.matmul(pb[ba + w],
                                        wv2[:, k, w * 128:(w + 1) * 128],
                                        xnf3[:, k, :],
                                        start=(k == 0), stop=(k == 15))
                return last
            sc.emit("tensor", fm, waits=sc.wall(sw, "sV", "sS"), inc=("sT", 1))
            if hs + 2 < NH:
                loadA(hs + 2)
            if len(pend) > 1:
                p = pend.pop(0)
                sc.emit("scalar", p[0], waits=p[1] + sc.wall("sV"),
                        inc=("sS", 1))
                sc.emit("vector", p[2], waits=sc.wall("sS"), inc=("sV", 1))
            def fsg(e, ba=ba):
                return e.activation(sgt, pb[ba], AF.Sigmoid)
            def fmu(e, hs=hs, ba=ba):
                return e.tensor_mul(hid3[:, hs, :], sgt, pb[ba + 1])
            pend.append((fsg, sc.wall("sT"), fmu))
        for p in pend:
            sc.emit("scalar", p[0], waits=p[1] + sc.wall("sV"), inc=("sS", 1))
            sc.emit("vector", p[2], waits=sc.wall("sS"), inc=("sV", 1))
        def loadB(oq, hg):
            nhc = 8 if hg < 5 else 4
            db = hg % 2
            st = "sTab0" if db == 0 else "sTab1"
            def fw(e, oq=oq, hg=hg, db=db, nhc=nhc):
                e.dma_start(
                    w3sv[db][:, 0:nhc, :],
                    w3qT[hg * 1024:hg * 1024 + nhc * 128,
                         oq * 512:(oq + 1) * 512]
                    .rearrange("(c p) j -> p c j", p=128)).then_inc(
                        sc.sems[st], 16)
                return None
            sc.bump(st, 16)
            sc.emit("sync", fw, waits=sc.wall("sT", "sQw0", "sQw1"))
        loadB(0, 0)
        loadB(0, 1)
        for oq in range(4):
            for hg in range(6):
                nhc = 8 if hg < 5 else 4
                db = hg % 2
                st = "sTab0" if db == 0 else "sTab1"
                def fm(e, oq=oq, hg=hg, db=db, nhc=nhc):
                    last = None
                    for i in range(nhc):
                        hc = hg * 8 + i
                        for tc in range(4):
                            last = e.matmul(pb[tc],
                                            hid3[:, hc, tc * 128:(tc + 1) * 128],
                                            w3sv[db][:, i, :],
                                            start=(hc == 0), stop=(hc == 43))
                    return last
                sc.emit("tensor", fm, waits=sc.wall(st, "sV", "sS"),
                        inc=("sT", 1))
                nx = oq * 6 + hg + 2
                if nx < 24:
                    loadB(nx // 6, nx % 6)
            sb = oq % 2
            for tc in range(4):
                def fe(e, oq=oq, tc=tc, sb=sb):
                    return e.tensor_copy(stg3[sb][:, tc, :], pb[tc])
                so = "sOut" if sb == 0 else "sOut1"
                sc.emit("vector", fe,
                        waits=sc.wall("sT") + [(so, stg_free[sb])],
                        inc=("sV", 1))
            def fstB(e, oq=oq, sb=sb, oc=oc):
                dst = Y[oc * C:(oc + 1) * C, oq * 512:(oq + 1) * 512]
                return e.dma_start(dst.rearrange("(c p) j -> p c j", p=128),
                                   stg3[sb], accum_op=ALU.add)
            so = "sOut" if sb == 0 else "sOut1"
            sc.emit("gpsimd", fstB, waits=sc.wall("sV"), inc=(so, 16))
            stg_free[sb] = sc.c(so)

    setup()
    qgen = quant_units() if do_ffn else None
    qdone = [not do_ffn]

    def drain_quant(n):
        if qdone[0]:
            return
        for _ in range(n):
            try:
                next(qgen)
            except StopIteration:
                qdone[0] = True
                return

    n_chunks = n_prior + n_own
    tconv = {"K": {}, "Q": {}}
    tcnt = [0]

    def conv_unit(kind, cc):
        sdc, sds, ddc, dds = ((TCK, TSK, TCKb, TSKb) if kind == "K"
                              else (TCQ, TSQ, TCQb, TSQb))
        def fc(e, cc=cc, sdc=sdc, sds=sds, ddc=ddc, dds=dds):
            e.dma_start(ddc[:, cc * C:(cc + 1) * C],
                        sdc[:, cc * C:(cc + 1) * C]).then_inc(
                            sc.sems["sTcv"], 16)
            e.dma_start(dds[:, cc * C:(cc + 1) * C],
                        sds[:, cc * C:(cc + 1) * C]).then_inc(
                            sc.sems["sTcv"], 16)
            return None
        sc.bump("sTcv", 32)
        sc.emit("gpsimd", fc)
        tcnt[0] += 32
        tconv[kind][cc] = tcnt[0]

    for ch in range(n_prior + n_own):
        own = ch >= n_prior
        oc = ch - n_prior
        if own and ch == n_prior:
            if do_ffn:
                drain_quant(10 ** 9)
                qfence = sc.wall("sQw0", "sQw1", "sQl0", "sQl1", "sG", "sQg")
                for eng in ("sync", "gpsimd", "tensor", "vector", "scalar"):
                    sc.emit(eng, lambda e: None, waits=qfence)
            def fyp(e):
                e.dma_start(Y, Xc[n_prior * C:, :]).then_inc(
                    sc.sems["sYP"], 16)
                return None
            sc.bump("sYP", 16)
            sc.emit("gpsimd", fyp)
        ln_chunk(ch)
        if ch == 0:
            def fpc(e):
                e.dma_start(WkB, Wk).then_inc(sc.sems["sPC"], 16)
                e.dma_start(WvB, Wv).then_inc(sc.sems["sPCv"], 16)
                e.dma_start(WqB, Wq).then_inc(sc.sems["sPCq"], 16)
                return None
            sc.bump("sPC", 16)
            sc.bump("sPCv", 16)
            sc.bump("sPCq", 16)
            sc.emit("gpsimd", fpc)
        if ch == 0:
            units = [("K", 0)] + ([("K", 1)] if n_chunks > 1 else [])
            units += [("Q", 0)]
        elif ch == 1:
            units = [("Q", q) for q in range(1, n_own)]
            units += [("K", 2)] if n_chunks > 2 else []
        else:
            units = [("K", ch + 1)] if ch + 1 < n_chunks else []
        for u in units:
            conv_unit(*u)
        if not own:
            drain_quant(2)
        if own:
            proj_rot(Qd3, WqB, TCQb, TSQb, tabq, oc * C, "sPCq",
                     tconv["Q"][oc])
        proj_rot(Kd3, WkB, TCKb, TSKb, tabk, ch * C, "sPC", tconv["K"][ch])
        if not own:
            drain_quant(3)
        proj_v()
        if not own:
            drain_quant(2)
        if own:
            scores()
            av_cross(oc)
        state_update(last=(own and ch == n_prior + n_own - 1))
        if not own:
            drain_quant(3)
        if own and do_ffn:
            ffn_chunk(oc)

    sc.emit("sync", lambda e: None,
            waits=[("sOut", sc.c("sOut")), ("sOut1", sc.c("sOut1"))])
    sc.build_block()
    return nc


_CACHE = {}


def kernel(L0_attn_X, L0_attn_W_Q, L0_attn_W_K, L0_attn_W_V,
           L0_ffn_w1, L0_ffn_w2, L0_ffn_w3):
    X = np.ascontiguousarray(np.asarray(L0_attn_X, dtype=np.float32))
    Wqh = np.ascontiguousarray(np.asarray(L0_attn_W_Q, dtype=np.float32)[:, PERM])
    Wkh = np.ascontiguousarray(np.asarray(L0_attn_W_K, dtype=np.float32)[:, PERM])
    Wvh = np.ascontiguousarray(np.asarray(L0_attn_W_V, dtype=np.float32))
    w1 = np.ascontiguousarray(np.asarray(L0_ffn_w1, dtype=np.float32))
    w2 = np.ascontiguousarray(np.asarray(L0_ffn_w2, dtype=np.float32))
    w3 = np.ascontiguousarray(np.asarray(L0_ffn_w3, dtype=np.float32))

    if "nc" not in _CACHE:
        _CACHE["nc"] = build_program(do_ffn=True, n_prior=NPRIOR, n_own=NCHUNK)
    nc = _CACHE["nc"]

    msk = host_masks()
    idn = np.eye(128, dtype=ml_dtypes.bfloat16)
    idg = (np.eye(128) * GC).astype(np.float32)

    in_maps = []
    for core in range(8):
        b = core // 4
        s = core % 4
        cq, sq, ck, sk = host_tables(s, NPRIOR, NCHUNK)
        own0 = s * SEG
        PT = NPRIOR * C
        if s == 0:
            xprior = np.zeros((PT, DM), dtype=np.float32)
        else:
            xprior = X[b, own0 - PT:own0]
        xctx = np.ascontiguousarray(
            np.concatenate([xprior, X[b, own0:own0 + SEG]], axis=0))
        in_maps.append(dict(Xc=xctx, Wq=Wqh, Wk=Wkh, Wv=Wvh,
                            TCQ=cq, TSQ=sq, TCK=ck, TSK=sk,
                            MSK=msk, IDN=idn, IDG=idg,
                            w1=w1, w2=w2, w3=w3))
    res = run_bass_kernel_spmd(nc, in_maps, core_ids=list(range(8)))
    out = np.empty((B, S, DM), dtype=res.results[0]["Y"].dtype)
    for core in range(8):
        b = core // 4
        s = core % 4
        out[b, s * SEG:(s + 1) * SEG] = res.results[core]["Y"]
    return out


# version 20 (final)
# speedup vs baseline: 1.0765x; 1.0765x over previous
import numpy as np
import ml_dtypes
import concourse.bass as bass
import concourse.mybir as mybir
from concourse.bass_utils import run_bass_kernel_spmd

F32 = mybir.dt.float32
BF16 = mybir.dt.bfloat16
AF = mybir.ActivationFunctionType
ALU = mybir.AluOpType
AX = mybir.AxisListType

GAMMA = 0.96875
DM = 2048
HALF = DM // 2
HD = 5632
SB = 512
C = 512
SEG = 2048
NCHUNK = SEG // C
NPRIOR = 2
B, S = 2, 8192
GC = float(GAMMA ** C)
QEPS = 1e-8
LNEPS = 1e-5

PERM = np.concatenate([np.arange(0, DM, 2), np.arange(1, DM, 2)])


def host_tables(seg_idx, n_prior, n_own):
    j = np.arange(HALF, dtype=np.float64)
    scale_vec = (2 * j + 0.4 * DM) / (1.4 * DM)
    inv_freq = 1.0 / 10000.0 ** (j / HALF)

    own0 = seg_idx * SEG
    own = np.arange(own0, own0 + n_own * C, dtype=np.float64)
    ctx0 = own0 - n_prior * C
    ctx = np.arange(ctx0, ctx0 + (n_prior + n_own) * C, dtype=np.float64)

    def mk(pos, up):
        a = np.mod(pos, C)
        ang = pos[None, :] * inv_freq[:, None]
        sc = scale_vec[:, None] ** (pos[None, :] / SB)
        if up:
            g = GAMMA ** (a + 1.0)
            s = sc
        else:
            g = GAMMA ** (-(a + 1.0))
            s = 1.0 / sc
        return ((np.cos(ang) * s * g[None, :]).astype(np.float32),
                (np.sin(ang) * s * g[None, :]).astype(np.float32))

    cq, sq = mk(own, True)
    ck, sk = mk(np.maximum(ctx, 0.0), False)
    return cq, sq, ck, sk


def host_masks():
    m = np.zeros((4 * 128, C), dtype=np.float32)
    for mc in range(4):
        ml = np.arange(128)[:, None] + mc * 128
        n = np.arange(C)[None, :]
        m[mc * 128:(mc + 1) * 128] = (n >= ml).astype(np.float32)
    return m.astype(ml_dtypes.bfloat16)


class Sched:
    ENGS = ("sync", "gpsimd", "tensor", "vector", "scalar")

    def __init__(self, nc):
        self.nc = nc
        self.ops = {e: [] for e in self.ENGS}
        self.sems = {}
        self.cnt = {}

    def sem(self, name):
        if name not in self.sems:
            self.sems[name] = self.nc.alloc_semaphore(name)
            self.cnt[name] = 0
        return self.sems[name]

    def emit(self, eng, fn, waits=(), inc=None):
        for (s, v) in waits:
            assert s in self.cnt and v <= self.cnt[s], (s, v, self.cnt.get(s))
        if inc:
            self.cnt[inc[0]] += inc[1]
        self.ops[eng].append((fn, tuple(waits), inc))

    def c(self, name):
        return self.cnt.get(name, 0)

    def bump(self, name, n):
        self.cnt[name] += n

    def wall(self, *names):
        return [(n, self.cnt[n]) for n in names if self.cnt.get(n, 0) > 0]

    def build_block(self):
        nc = self.nc
        with nc.Block() as blk:
            for eng in self.ENGS:
                ops = list(self.ops[eng])

                def mk(ops):
                    def run(e):
                        for fn, waits, inc in ops:
                            for (s, v) in waits:
                                e.wait_ge(self.sems[s], v)
                            last = fn(e)
                            if inc is not None:
                                assert last is not None
                                last.then_inc(self.sems[inc[0]], inc[1])
                    return run
                getattr(blk, eng)(mk(ops))


def build_program(do_ffn=True, n_prior=NPRIOR, n_own=NCHUNK):
    nc = bass.Bass("TRN2", target_bir_lowering=False, debug=False)
    TOK_CTX = (n_prior + n_own) * C

    Xc = nc.dram_tensor("Xc", [TOK_CTX, DM], F32, kind="ExternalInput").ap()
    Wq = nc.dram_tensor("Wq", [DM, DM], F32, kind="ExternalInput").ap()
    Wk = nc.dram_tensor("Wk", [DM, DM], F32, kind="ExternalInput").ap()
    Wv = nc.dram_tensor("Wv", [DM, DM], F32, kind="ExternalInput").ap()
    TCQ = nc.dram_tensor("TCQ", [HALF, n_own * C], F32, kind="ExternalInput").ap()
    TSQ = nc.dram_tensor("TSQ", [HALF, n_own * C], F32, kind="ExternalInput").ap()
    TCK = nc.dram_tensor("TCK", [HALF, TOK_CTX], F32, kind="ExternalInput").ap()
    TSK = nc.dram_tensor("TSK", [HALF, TOK_CTX], F32, kind="ExternalInput").ap()
    MSK = nc.dram_tensor("MSK", [4 * 128, C], BF16, kind="ExternalInput").ap()
    IDN = nc.dram_tensor("IDN", [128, 128], BF16, kind="ExternalInput").ap()
    IDG = nc.dram_tensor("IDG", [128, 128], F32, kind="ExternalInput").ap()
    Y = nc.dram_tensor("Y", [n_own * C, DM], F32, kind="ExternalOutput").ap()
    WqB = nc.dram_tensor("WqB", [DM, DM], BF16, kind="Internal").ap()
    WkB = nc.dram_tensor("WkB", [DM, DM], BF16, kind="Internal").ap()
    WvB = nc.dram_tensor("WvB", [DM, DM], BF16, kind="Internal").ap()
    TCQb = nc.dram_tensor("TCQb", [HALF, n_own * C], BF16, kind="Internal").ap()
    TSQb = nc.dram_tensor("TSQb", [HALF, n_own * C], BF16, kind="Internal").ap()
    TCKb = nc.dram_tensor("TCKb", [HALF, TOK_CTX], BF16, kind="Internal").ap()
    TSKb = nc.dram_tensor("TSKb", [HALF, TOK_CTX], BF16, kind="Internal").ap()
    if do_ffn:
        w1d = nc.dram_tensor("w1", [HD, DM], F32, kind="ExternalInput").ap()
        w2d = nc.dram_tensor("w2", [HD, DM], F32, kind="ExternalInput").ap()
        w3d = nc.dram_tensor("w3", [DM, HD], F32, kind="ExternalInput").ap()
        w12qT = nc.dram_tensor("w12qT", [DM, 2 * HD], BF16, kind="Internal").ap()
        w3qT = nc.dram_tensor("w3qT", [HD, DM], BF16, kind="Internal").ap()
        gtmp = nc.dram_tensor("gtmp", [3, 128], F32, kind="Internal").ap()
        gthr = nc.dram_tensor("gthr", [1, 4], F32, kind="Internal").ap()

    ARENA = 210176
    nc.alloc_sbuf_tensor("arena", [128, ARENA], mybir.dt.uint8)
    ABASE = nc.lookup_mloc("arena").addr
    off = [0]

    def at(name, shape, dtype, o=None):
        nbytes = (shape[1] * (4 if dtype == F32 else 2) + 31) // 32 * 32
        if o is None:
            o = off[0]
            off[0] += nbytes
        assert o + nbytes <= ARENA and o % 32 == 0, (name, o, nbytes)
        return nc.alloc_sbuf_tensor_at(name, shape, dtype,
                                       offset=ABASE + o).ap()

    Rst = at("Rst", [128, 16 * DM], BF16)
    idn = at("idn", [128, 128], BF16)
    idg = at("idg", [128, 128], BF16)
    stats = at("stats", [128, 32], F32)
    thr = at("thr", [128, 8], F32)
    masks = at("masks", [128, 4 * C], BF16)
    stg = [at(f"stg{i}", [128, 4 * 512], BF16) for i in range(2)]
    wsl2 = [at(f"wsl2{i}", [128, 16 * 128], BF16) for i in range(2)]
    xnf = at("xnf", [128, 16 * C], BF16)
    AB = off[0]
    tabq = [at(f"tabq{i}", [128, 2 * C], BF16) for i in range(2)]
    tabk = [at(f"tabk{i}", [128, 2 * C], BF16) for i in range(2)]
    Qd = at("Qd", [128, 16 * C], BF16)
    Kd = at("Kd", [128, 16 * C], BF16)
    Ktm = at("Ktm", [128, 4 * C], BF16)
    Vtm = at("Vtm", [128, 4 * DM], BF16)
    att = at("att", [128, 4 * C], BF16)
    wsl = [at(f"wsl{i}", [128, 16 * 128], BF16) for i in range(2)] + wsl2
    wvs = [at(f"wvs{i}", [128, 8 * C], BF16) for i in range(2)]
    RT0 = off[0]
    rtmp0 = at("rtmp0", [128, C], F32)
    rtmp1 = at("rtmp1", [128, C], F32)
    sqdump = at("sqdump", [128, DM], BF16, RT0)
    xsl = at("xsl", [128, DM], BF16)
    XA = off[0]
    xa = at("xa", [128, DM], BF16)
    xb = at("xb", [128, DM], BF16)
    AE = off[0]
    assert AE <= ARENA, AE
    if do_ffn:
        o = AB
        hid = at("hid", [128, 44 * C], BF16, o); o += 44 * C * 2
        w12 = []
        for i in range(2):
            w12.append(at(f"w12{i}", [128, 2 * 16 * 128], BF16, o))
            o += 2 * 16 * 128 * 2
        w3s = []
        for i in range(2):
            w3s.append(at(f"w3s{i}", [128, 8 * C], BF16, o)); o += 8 * C * 2
        sgt = at("sgt", [128, C], F32, o); o += C * 4
        assert o <= AE, (o, AE)
        QB = AB + 4 * (2 * C * 2)
        qtile = at("qtile", [128, DM], F32, QB)
        qtmp = at("qtmp", [128, DM], F32, QB + DM * 4)
        attoff = QB + 16 * C * 2 * 2 + 4 * C * 2 + 4 * DM * 2
        qout = at("qout", [128, DM], BF16, attoff)
        trv = at("trv", [128, 2048], BF16, AB)

    P = lambda n, s, d: nc.alloc_psum_tensor(n, s, d).ap()
    pb = [P(f"pb{i}", [128, 512], F32) for i in range(6)]
    tb = [P(f"tb{i}", [128, 1024], BF16) for i in range(2)]

    sc = Sched(nc)
    for s in ("sX", "sTab0", "sTab1", "sW0", "sW1", "sWv0", "sWv1", "sOut",
              "sT", "sV", "sS", "sG", "sQw0", "sQw1", "sQl0", "sQl1", "sIni",
              "sF0", "sF1", "sPC", "sQv0", "sQv1", "sQg", "sYP", "sPCv", "sPCq", "sOut1", "sTg0", "sTg1", "sIdg", "sTcv"):
        sc.sem(s)

    xnf3 = xnf.rearrange("p (c j) -> p c j", c=16)
    stg3 = [s.rearrange("p (c j) -> p c j", c=4) for s in stg]
    Qd3 = Qd.rearrange("p (c j) -> p c j", c=16)
    Kd3 = Kd.rearrange("p (c j) -> p c j", c=16)
    Ktm3 = Ktm.rearrange("p (c j) -> p c j", c=4)
    Vtm3 = Vtm.rearrange("p (c j) -> p c j", c=4)
    att3 = att.rearrange("p (c j) -> p c j", c=4)
    Rst3 = Rst.rearrange("p (c j) -> p c j", c=16)
    wsl3 = [w.rearrange("p (c j) -> p c j", c=16) for w in wsl]
    wvs3 = [w.rearrange("p (c j) -> p c j", c=8) for w in wvs]
    msk3 = masks.rearrange("p (c j) -> p c j", c=4)
    if do_ffn:
        hid3 = hid.rearrange("p (c j) -> p c j", c=44)
        w3sv = [w.rearrange("p (c j) -> p c j", c=8) for w in w3s]

    def setup():
        def f(e):
            for mc in range(4):
                e.dma_start(msk3[:, mc, :],
                            MSK[mc * 128:(mc + 1) * 128, :]).then_inc(
                                sc.sems["sIni"], 16)
            e.dma_start(idn, IDN).then_inc(sc.sems["sIni"], 16)
            return None
        sc.bump("sIni", 80)
        sc.emit("sync", f)
        def fidg(e):
            e.dma_start(idg, IDG).then_inc(sc.sems["sIdg"], 16)
            return None
        sc.bump("sIdg", 16)
        sc.emit("gpsimd", fidg)
        for eng in ("tensor", "vector", "scalar"):
            sc.emit(eng, lambda e: None, waits=[("sIni", 80), ("sIdg", 16)])
        sc.emit("vector", lambda e: e.memset(Rst, 0.0), inc=("sV", 1))

    lnx = [0]

    def ln_chunk(ch):
        slabs = [xa, xb]
        def load(tc):
            def fxl(e, tc=tc):
                return e.dma_start(slabs[tc % 2],
                                   Xc[ch * C + tc * 128:ch * C + (tc + 1) * 128, :])
            lnx[0] += 1
            w = sc.wall("sV") if tc >= 2 else sc.wall("sV", "sS")
            sc.emit("gpsimd", fxl, waits=w, inc=("sX", 16))
        load(0)
        load(1)
        for tc in range(4):
            sl = slabs[tc % 2]
            def fsq(e, tc=tc, sl=sl):
                e.drain()
                return e.activation(sqdump, sl, AF.Square,
                                    accum_out=stats[:, 20 + tc:21 + tc])
            sc.emit("scalar", fsq,
                    waits=sc.wall("sV") + [("sX", sc.c("sX"))],
                    inc=("sS", 1))
            sb = 5 * tc
            def fstat(e, tc=tc, sb=sb, sl=sl):
                e.drain()
                e.tensor_reduce(stats[:, sb:sb + 1], sl, AX.X, ALU.add)
                e.drain()
                e.tensor_scalar(stats[:, sb + 1:sb + 2], stats[:, sb:sb + 1],
                                1.0 / DM, None, ALU.mult)
                e.tensor_scalar(stats[:, sb + 2:sb + 3],
                                stats[:, 20 + tc:21 + tc], 1.0 / DM, None,
                                ALU.mult)
                e.drain()
                e.tensor_mul(stats[:, sb + 3:sb + 4], stats[:, sb + 1:sb + 2],
                             stats[:, sb + 1:sb + 2])
                e.drain()
                e.tensor_sub(stats[:, sb + 3:sb + 4], stats[:, sb + 2:sb + 3],
                             stats[:, sb + 3:sb + 4])
                e.drain()
                return e.tensor_scalar(stats[:, sb + 3:sb + 4],
                                       stats[:, sb + 3:sb + 4], LNEPS, None,
                                       ALU.add)
            sc.emit("vector", fstat, waits=sc.wall("sS"), inc=("sV", 1))
            def fsqrt(e, sb=sb):
                e.drain()
                return e.activation(stats[:, sb + 4:sb + 5],
                                    stats[:, sb + 3:sb + 4], AF.Sqrt)
            sc.emit("scalar", fsqrt, waits=sc.wall("sV"), inc=("sS", 1))
            def fnorm(e, tc=tc, sb=sb, sl=sl):
                e.drain()
                e.reciprocal(stats[:, 25:26], stats[:, sb + 4:sb + 5])
                e.drain()
                e.tensor_mul(stats[:, 26:27], stats[:, 25:26], stats[:, 25:26])
                e.drain()
                e.tensor_mul(stats[:, 26:27], stats[:, 26:27],
                             stats[:, sb + 3:sb + 4])
                e.drain()
                e.tensor_scalar(stats[:, 26:27], stats[:, 26:27], -0.5, 1.5,
                                ALU.mult, ALU.add)
                e.drain()
                e.tensor_mul(stats[:, 25:26], stats[:, 25:26], stats[:, 26:27])
                e.drain()
                e.tensor_mul(stats[:, 27:28], stats[:, sb + 1:sb + 2],
                             stats[:, 25:26])
                e.drain()
                return e.tensor_scalar(xsl, sl, stats[:, 25:26],
                                       stats[:, 27:28], ALU.mult, ALU.subtract)
            sc.emit("vector", fnorm, waits=sc.wall("sS", "sT"), inc=("sV", 1))
            if tc + 2 < 4:
                load(tc + 2)
            for half in range(2):
                def ftr(e, half=half):
                    last = None
                    for i in range(8):
                        dc = half * 8 + i
                        last = e.transpose(tb[half][:, i * 128:(i + 1) * 128],
                                           xsl[:, dc * 128:(dc + 1) * 128], idn)
                    return last
                sc.emit("tensor", ftr,
                        waits=sc.wall("sV", "sS", "sQw0", "sQw1"),
                        inc=("sT", 1))
                def fev(e, tc=tc, half=half):
                    src = tb[half].rearrange("p (c j) -> p c j", c=8)
                    dst = xnf3[:, half * 8:half * 8 + 8, tc * 128:tc * 128 + 128]
                    return e.copy(dst, src)
                sc.emit("scalar", fev, waits=sc.wall("sT"), inc=("sS", 1))

    def proj_rot(dst3, WB, TC, TS, tabs, col0, pcw, tv):
        def loadp(p):
            b0, b1 = 2 * (p % 2), 2 * (p % 2) + 1
            def fwa(e, p=p, b0=b0):
                e.dma_start(wsl3[b0], WB[:, p * 128:(p + 1) * 128]
                            .rearrange("(c p) j -> p c j", p=128)).then_inc(
                                sc.sems["sW0"], 16)
                return None
            sc.bump("sW0", 16)
            sc.emit("sync", fwa, waits=sc.wall("sT") + [(pcw, 16)])
            def fwb(e, p=p, b1=b1):
                e.dma_start(wsl3[b1], WB[:, (p + 8) * 128:(p + 9) * 128]
                            .rearrange("(c p) j -> p c j", p=128)).then_inc(
                                sc.sems["sW1"], 16)
                return None
            sc.bump("sW1", 16)
            sc.emit("sync", fwb, waits=sc.wall("sT") + [(pcw, 16)])
        loadp(0)
        loadp(1)
        pend = None
        for p in range(8):
            par = p % 2
            b0, b1 = (0, 1) if par == 0 else (2, 3)
            st = "sTg0" if par == 0 else "sTg1"
            def ft(e, p=p, st=st, tab=tabs[par]):
                e.dma_start(tab[:, 0:C],
                            TC[p * 128:(p + 1) * 128, col0:col0 + C]).then_inc(
                                sc.sems[st], 16)
                e.dma_start(tab[:, C:2 * C],
                            TS[p * 128:(p + 1) * 128, col0:col0 + C]).then_inc(
                                sc.sems[st], 16)
                return None
            sc.bump(st, 32)
            sc.emit("gpsimd", ft,
                    waits=sc.wall("sV", "sQw0", "sQw1") + [("sTcv", tv)])
            for h, bk in ((0, b0), (1, b1)):
                sw = "sW0" if h == 0 else "sW1"
                wb = 2 * par + h
                def fm(e, wb=wb, bk=bk):
                    last = None
                    for k in range(16):
                        last = e.matmul(pb[bk], wsl3[wb][:, k, :],
                                        xnf3[:, k, :],
                                        start=(k == 0), stop=(k == 15))
                    return last
                sc.emit("tensor", fm, waits=sc.wall(sw, "sS", "sV"),
                        inc=("sT", 1))
            if p + 2 < 8:
                loadp(p + 2)
            if pend is not None:
                sc.emit("vector", pend[0], waits=pend[1], inc=("sV", 1))
            def fr(e, p=p, tab=tabs[par], b0=b0, b1=b1):
                e.drain()
                cT = tab[:, 0:C]
                sT = tab[:, C:2 * C]
                e.tensor_mul(rtmp0, pb[b0], cT)
                e.tensor_mul(rtmp1, pb[b1], sT)
                e.drain()
                e.tensor_sub(dst3[:, p, :], rtmp0, rtmp1)
                e.drain()
                e.tensor_mul(rtmp0, pb[b1], cT)
                e.tensor_mul(rtmp1, pb[b0], sT)
                e.drain()
                return e.tensor_add(dst3[:, p + 8, :], rtmp0, rtmp1)
            pend = (fr, sc.wall("sT", "sS", st))
        sc.emit("vector", pend[0], waits=pend[1], inc=("sV", 1))

    def proj_v():
        seqs = [(vs, dh) for vs in range(4) for dh in range(2)]
        def mkload(i):
            vs, dh = seqs[i]
            buf = i % 2
            swn = "sWv0" if buf == 0 else "sWv1"
            def fw(e, vs=vs, dh=dh, buf=buf, swn=swn):
                src = WvB[dh * 1024:(dh + 1) * 1024, vs * 512:(vs + 1) * 512]
                e.dma_start(wvs3[buf], src.rearrange("(c p) j -> p c j",
                                                     p=128)).then_inc(
                                                         sc.sems[swn], 16)
                return None
            sc.bump(swn, 16)
            sc.emit("sync", fw, waits=sc.wall("sT") + [("sPCv", 16)])
        mkload(0)
        mkload(1)
        for i, (vs, dh) in enumerate(seqs):
            buf = i % 2
            swn = "sWv0" if buf == 0 else "sWv1"
            def fm(e, vs=vs, dh=dh, buf=buf):
                last = None
                for tc in range(4):
                    for k in range(8):
                        dc = dh * 8 + k
                        last = e.matmul(pb[tc],
                                        xnf3[:, dc, tc * 128:(tc + 1) * 128],
                                        wvs3[buf][:, k, :],
                                        start=(dc == 0), stop=(dc == 15))
                return last
            sc.emit("tensor", fm, waits=sc.wall(swn, "sS", "sV"), inc=("sT", 1))
            if i + 2 < len(seqs):
                mkload(i + 2)
            if dh == 1:
                for tc in range(4):
                    def fe(e, vs=vs, tc=tc):
                        return e.copy(Vtm3[:, tc, vs * 512:(vs + 1) * 512],
                                      pb[tc])
                    sc.emit("scalar", fe, waits=sc.wall("sT"), inc=("sS", 1))

    def scores():
        for mc in range(4):
            def fm(e, mc=mc):
                last = None
                for k in range(16):
                    last = e.matmul(pb[mc], Kd3[:, k, mc * 128:(mc + 1) * 128],
                                    Qd3[:, k, :], start=(k == 0), stop=(k == 15))
                return last
            sc.emit("tensor", fm, waits=sc.wall("sV", "sS"), inc=("sT", 1))
        for mc in range(4):
            def fk(e, mc=mc):
                return e.tensor_mul(att3[:, mc, :], pb[mc], msk3[:, mc, :])
            sc.emit("vector", fk, waits=sc.wall("sT"), inc=("sV", 1))

    stg_free = [0, 0]

    def av_cross(oc):
        pend = []
        tiles = [(vs, ncb) for vs in range(4) for ncb in range(4)]
        for i, (vs, ncb) in enumerate(tiles):
            bank = pb[i % 6]
            def fm(e, ncb=ncb, vs=vs, bank=bank):
                first = True
                last = None
                for mc in range(ncb + 1):
                    last = e.matmul(bank, att3[:, mc, ncb * 128:(ncb + 1) * 128],
                                    Vtm3[:, mc, vs * 512:(vs + 1) * 512],
                                    start=first, stop=False)
                    first = False
                for k in range(16):
                    last = e.matmul(bank, Qd3[:, k, ncb * 128:(ncb + 1) * 128],
                                    Rst3[:, k, vs * 512:(vs + 1) * 512],
                                    start=False, stop=(k == 15))
                return last
            sc.emit("tensor", fm, waits=sc.wall("sV", "sS"), inc=("sT", 1))
            sb = vs % 2
            def fe(e, ncb=ncb, vs=vs, bank=bank, sb=sb):
                return e.tensor_copy(stg3[sb][:, ncb, :], bank)
            so = "sOut" if sb == 0 else "sOut1"
            pend.append((fe, sc.wall("sT") + [(so, stg_free[sb]),
                                              ("sYP", 16)]))
            if len(pend) > 5:
                f, w = pend.pop(0)
                sc.emit("vector", f, waits=w, inc=("sV", 1))
            if ncb == 3:
                while pend:
                    f, w = pend.pop(0)
                    sc.emit("vector", f, waits=w, inc=("sV", 1))
                def fst(e, vs=vs, oc=oc, sb=sb):
                    dst = Y[oc * C:(oc + 1) * C, vs * 512:(vs + 1) * 512]
                    return e.dma_start(dst.rearrange("(c p) j -> p c j", p=128),
                                       stg3[sb], accum_op=ALU.add)
                so = "sOut" if sb == 0 else "sOut1"
                sc.emit("gpsimd", fst, waits=sc.wall("sV"), inc=(so, 16))
                stg_free[sb] = sc.c(so)

    def state_update(last=False):
        if last:
            return
        pend = []
        for js in range(4):
            for g in range(2):
                def ftr(e, js=js, g=g):
                    last = None
                    for i in range(2):
                        mc = g * 2 + i
                        for jj in range(4):
                            jc = js * 4 + jj
                            last = e.transpose(
                                tb[g][:, (i * 4 + jj) * 128:(i * 4 + jj + 1) * 128],
                                Kd3[:, jc, mc * 128:(mc + 1) * 128], idn)
                    return last
                sc.emit("tensor", ftr,
                        waits=sc.wall("sV", "sS", "sQw0", "sQw1"),
                        inc=("sT", 1))
                def fev(e, g=g):
                    src = tb[g].rearrange("p (c j) -> p c j", c=2)
                    dst = Ktm3[:, g * 2:g * 2 + 2, :]
                    return e.tensor_scalar(dst, src, GC, None, ALU.mult)
                sc.emit("vector", fev, waits=sc.wall("sT"), inc=("sV", 1))
            for jj in range(4):
                jc = js * 4 + jj
                for vs in range(4):
                    i = js * 16 + jj * 4 + vs
                    bank = pb[i % 6]
                    def fm(e, jc=jc, jj=jj, vs=vs, bank=bank):
                        for mc in range(4):
                            e.matmul(bank, Ktm3[:, mc, jj * 128:(jj + 1) * 128],
                                     Vtm3[:, mc, vs * 512:(vs + 1) * 512],
                                     start=(mc == 0), stop=False)
                        return e.matmul(bank, idg,
                                        Rst3[:, jc, vs * 512:(vs + 1) * 512],
                                        start=False, stop=True)
                    sc.emit("tensor", fm, waits=sc.wall("sV", "sS"),
                            inc=("sT", 1))
                    def fe(e, jc=jc, vs=vs, bank=bank):
                        return e.tensor_copy(Rst3[:, jc, vs * 512:(vs + 1) * 512],
                                             bank)
                    pend.append((fe, sc.wall("sT")))
                    if len(pend) > 5:
                        f, w = pend.pop(0)
                        sc.emit("vector", f, waits=w, inc=("sV", 1))
        for f, w in pend:
            sc.emit("vector", f, waits=w, inc=("sV", 1))

    def quant_units():
        specs = [(w1d, w12qT, HD, 0), (w2d, w12qT, HD, 1), (w3d, w3qT, DM, 2)]
        qv = [0, 0]
        ql = [0, 0]
        it = 0
        for (wd, _, nrow, wi) in specs:
            nt = nrow // 128
            ncols = DM if wi < 2 else HD
            cbs = [(c0, min(DM, ncols - c0)) for c0 in range(0, ncols, DM)]
            for t in range(nt):
                for (c0, cw) in cbs:
                    par = it % 2
                    it += 1
                    first = (t == 0 and c0 == 0)
                    slq = "sQl0" if par == 0 else "sQl1"
                    sqv = "sQv0" if par == 0 else "sQv1"
                    u = ql[par]
                    wv = [(sqv, u)] if u > 0 else []
                    def fl(e, wd=wd, t=t, par=par, c0=c0, cw=cw):
                        buf = qtile if par == 0 else qtmp
                        return e.dma_start(buf[:, 0:cw],
                                           wd[t * 128:(t + 1) * 128,
                                              c0:c0 + cw])
                    ql[par] += 1
                    sc.emit("sync", fl, waits=wv, inc=(slq, 16))
                    def fr(e, first=first, par=par, wi=wi, cw=cw):
                        srcb = qtile if par == 0 else qtmp
                        e.drain()
                        e.tensor_reduce(stats[:, 31:32], srcb[:, 0:cw], AX.X,
                                        ALU.add, apply_absolute_value=True)
                        e.drain()
                        if first:
                            return e.tensor_copy(stats[:, 28 + wi:29 + wi],
                                                 stats[:, 31:32])
                        return e.tensor_add(stats[:, 28 + wi:29 + wi],
                                            stats[:, 28 + wi:29 + wi],
                                            stats[:, 31:32])
                    qv[par] += 1
                    sc.emit("vector", fr, waits=[(slq, ql[par] * 16)],
                            inc=(sqv, 1))
                    if it % 8 == 0:
                        yield
        def fd(e):
            for wi in range(3):
                e.dma_start(gtmp[wi, :],
                            stats[:, 28 + wi:29 + wi]).then_inc(
                                sc.sems["sG"], 16)
            return None
        sc.bump("sG", 48)
        sc.emit("sync", fd, waits=[("sQv0", qv[0]), ("sQv1", qv[1])])
        def fl2(e):
            return e.dma_start(qtmp[0:3, 0:128], gtmp)
        sc.emit("sync", fl2, waits=[("sG", sc.c("sG"))], inc=("sG", 16))
        def fred(e):
            e.drain()
            e.tensor_reduce(stats[0:3, 24:25], qtmp[0:3, 0:128], AX.X, ALU.add)
            e.drain()
            return e.tensor_scalar(stats[0:3, 24:25], stats[0:3, 24:25],
                                   0.5 / (HD * DM), 0.5 * QEPS, ALU.mult,
                                   ALU.add)
        qv[0] += 1
        sc.emit("vector", fred, waits=[("sG", sc.c("sG"))], inc=("sQv0", 1))
        def fst(e):
            for wi in range(3):
                e.dma_start(gthr[0:1, wi:wi + 1],
                            stats[wi:wi + 1, 24:25]).then_inc(sc.sems["sG"], 16)
            return None
        sc.bump("sG", 48)
        sc.emit("sync", fst, waits=[("sQv0", qv[0])])
        def fbc(e):
            return e.dma_start(thr[:, 0:3], gthr[0:1, 0:3].broadcast_to((128, 3)))
        sc.emit("sync", fbc, waits=[("sG", sc.c("sG"))], inc=("sG", 16))
        def fneg(e):
            e.drain()
            return e.tensor_scalar(thr[:, 4:7], thr[:, 0:3], -1.0, None,
                                   ALU.mult)
        qv[0] += 1
        sc.emit("vector", fneg, waits=[("sG", sc.c("sG"))], inc=("sQv0", 1))
        thrdone = [("sQv0", qv[0])]
        qg = [0]
        yield
        last_ftr_sT = 0
        fq_cnt = {0: 0, 1: 0}
        u = 0
        for (wd, wqT, nrow, wi) in specs:
            nt = nrow // 128
            ncols = DM if wi < 2 else HD
            cbs = [(c0, min(DM, ncols - c0)) for c0 in range(0, ncols, DM)]
            for t in range(nt):
                for (c0, cw) in cbs:
                    nb = cw // 128
                    par = u % 2
                    u += 1
                    slq = "sQl0" if par == 0 else "sQl1"
                    wv = ([("sQv0", fq_cnt[par])] if fq_cnt[par] else
                          [("sQv0", qv[0]), ("sQv1", qv[1])])
                    def fl(e, wd=wd, t=t, c0=c0, cw=cw, par=par):
                        buf = qtile if par == 0 else qtmp
                        return e.dma_start(buf[:, 0:cw],
                                           wd[t * 128:(t + 1) * 128,
                                              c0:c0 + cw])
                    sc.emit("sync", fl, waits=wv, inc=(slq, 16))
                    ql[par] += 1
                    def fqv(e, wi=wi, cw=cw, par=par):
                        buf = qtile if par == 0 else qtmp
                        e.drain()
                        e.tensor_single_scalar(qout[:, 0:cw], buf[:, 0:cw],
                                               thr[:, wi:wi + 1], ALU.is_gt)
                        e.drain()
                        e.scalar_tensor_tensor(qout[:, 0:cw], buf[:, 0:cw],
                                               thr[:, 4 + wi:5 + wi],
                                               qout[:, 0:cw],
                                               ALU.is_ge, ALU.add)
                        e.drain()
                        return e.tensor_scalar(qout[:, 0:cw], qout[:, 0:cw],
                                               1.0, None, ALU.subtract)
                    qv[0] += 1
                    fq_cnt[par] = qv[0]
                    sc.emit("vector", fqv,
                            waits=[(slq, ql[par] * 16),
                                   ("sT", min(last_ftr_sT, sc.c("sT")))]
                            + thrdone, inc=("sQv0", 1))
                    for bb in range(0, nb, 8):
                        g = (bb // 8) % 2
                        sq_ = "sQw0" if g == 0 else "sQw1"
                        n8 = min(8, nb - bb)
                        def ftr(e, bb=bb, g=g, n8=n8):
                            last = None
                            for i in range(n8):
                                last = e.transpose(
                                    tb[g][:, i * 128:(i + 1) * 128],
                                    qout[:, (bb + i) * 128:(bb + i + 1) * 128],
                                    idn)
                            return last
                        sc.emit("tensor", ftr,
                                waits=[("sQv0", qv[0])] + sc.wall("sS", sq_),
                                inc=("sT", 1))
                        my_sT = sc.c("sT")
                        def fstw(e, t=t, c0=c0, bb=bb, g=g, n8=n8,
                                 wqT=wqT, wi=wi):
                            tv = trv[:, g * 1024:g * 1024 + n8 * 128]
                            e.copy(tv, tb[g][:, 0:n8 * 128])
                            e.drain()
                            src = tv.rearrange("p (c j) -> p c j", c=n8)
                            r0 = c0 + bb * 128
                            if wi < 2:
                                cc = (2 * t + wi) * 128
                            else:
                                cc = t * 128
                            dst = wqT[r0:r0 + n8 * 128, cc:cc + 128]
                            dst = dst.rearrange("(c p) j -> p c j", p=128)
                            return e.dma_start(dst, src)
                        sc.emit("scalar", fstw,
                                waits=[("sT", my_sT)] + sc.wall(sq_),
                                inc=(sq_, 16))
                    last_ftr_sT = sc.c("sT")
                    if t % 4 == 3:
                        yield

    def ffn_chunk(oc):
        NH = HD // 128
        def loadA(hs):
            db = hs % 2
            sw = "sF0" if db == 0 else "sF1"
            def fwA(e, hs=hs, db=db, sw=sw):
                e.dma_start(w12[db].rearrange("p (c j) -> p c j", c=16),
                            w12qT[:, hs * 256:(hs + 1) * 256]
                            .rearrange("(c p) j -> p c j", p=128)).then_inc(
                                sc.sems[sw], 16)
                return None
            sc.bump(sw, 16)
            sc.emit("sync", fwA, waits=sc.wall("sT", "sQw0", "sQw1"))
        loadA(0)
        loadA(1)
        pend = []
        for hs in range(NH):
            db = hs % 2
            sw = "sF0" if db == 0 else "sF1"
            ba = 2 * (hs % 3)
            def fm(e, db=db, ba=ba):
                wv2 = w12[db].rearrange("p (c j) -> p c j", c=16)
                last = None
                for w in range(2):
                    for k in range(16):
                        last = e.matmul(pb[ba + w],
                                        wv2[:, k, w * 128:(w + 1) * 128],
                                        xnf3[:, k, :],
                                        start=(k == 0), stop=(k == 15))
                return last
            sc.emit("tensor", fm, waits=sc.wall(sw, "sV", "sS"), inc=("sT", 1))
            if hs + 2 < NH:
                loadA(hs + 2)
            if len(pend) > 1:
                p = pend.pop(0)
                sc.emit("scalar", p[0], waits=p[1] + sc.wall("sV"),
                        inc=("sS", 1))
                sc.emit("vector", p[2], waits=sc.wall("sS"), inc=("sV", 1))
            def fsg(e, ba=ba):
                return e.activation(sgt, pb[ba], AF.Sigmoid)
            def fmu(e, hs=hs, ba=ba):
                return e.tensor_mul(hid3[:, hs, :], sgt, pb[ba + 1])
            pend.append((fsg, sc.wall("sT"), fmu))
        for p in pend:
            sc.emit("scalar", p[0], waits=p[1] + sc.wall("sV"), inc=("sS", 1))
            sc.emit("vector", p[2], waits=sc.wall("sS"), inc=("sV", 1))
        def loadB(oq, hg):
            nhc = 8 if hg < 5 else 4
            db = hg % 2
            st = "sTab0" if db == 0 else "sTab1"
            def fw(e, oq=oq, hg=hg, db=db, nhc=nhc):
                e.dma_start(
                    w3sv[db][:, 0:nhc, :],
                    w3qT[hg * 1024:hg * 1024 + nhc * 128,
                         oq * 512:(oq + 1) * 512]
                    .rearrange("(c p) j -> p c j", p=128)).then_inc(
                        sc.sems[st], 16)
                return None
            sc.bump(st, 16)
            sc.emit("sync", fw, waits=sc.wall("sT", "sQw0", "sQw1"))
        loadB(0, 0)
        loadB(0, 1)
        for oq in range(4):
            for hg in range(6):
                nhc = 8 if hg < 5 else 4
                db = hg % 2
                st = "sTab0" if db == 0 else "sTab1"
                def fm(e, oq=oq, hg=hg, db=db, nhc=nhc):
                    last = None
                    for i in range(nhc):
                        hc = hg * 8 + i
                        for tc in range(4):
                            last = e.matmul(pb[tc],
                                            hid3[:, hc, tc * 128:(tc + 1) * 128],
                                            w3sv[db][:, i, :],
                                            start=(hc == 0), stop=(hc == 43))
                    return last
                sc.emit("tensor", fm, waits=sc.wall(st, "sV", "sS"),
                        inc=("sT", 1))
                nx = oq * 6 + hg + 2
                if nx < 24:
                    loadB(nx // 6, nx % 6)
            sb = oq % 2
            for tc in range(4):
                def fe(e, oq=oq, tc=tc, sb=sb):
                    return e.tensor_copy(stg3[sb][:, tc, :], pb[tc])
                so = "sOut" if sb == 0 else "sOut1"
                sc.emit("vector", fe,
                        waits=sc.wall("sT") + [(so, stg_free[sb])],
                        inc=("sV", 1))
            def fstB(e, oq=oq, sb=sb, oc=oc):
                dst = Y[oc * C:(oc + 1) * C, oq * 512:(oq + 1) * 512]
                return e.dma_start(dst.rearrange("(c p) j -> p c j", p=128),
                                   stg3[sb], accum_op=ALU.add)
            so = "sOut" if sb == 0 else "sOut1"
            sc.emit("gpsimd", fstB, waits=sc.wall("sV"), inc=(so, 16))
            stg_free[sb] = sc.c(so)

    setup()
    qgen = quant_units() if do_ffn else None
    qdone = [not do_ffn]

    def drain_quant(n):
        if qdone[0]:
            return
        for _ in range(n):
            try:
                next(qgen)
            except StopIteration:
                qdone[0] = True
                return

    n_chunks = n_prior + n_own
    tconv = {"K": {}, "Q": {}}
    tcnt = [0]

    def conv_unit(kind, cc):
        sdc, sds, ddc, dds = ((TCK, TSK, TCKb, TSKb) if kind == "K"
                              else (TCQ, TSQ, TCQb, TSQb))
        def fc(e, cc=cc, sdc=sdc, sds=sds, ddc=ddc, dds=dds):
            e.dma_start(ddc[:, cc * C:(cc + 1) * C],
                        sdc[:, cc * C:(cc + 1) * C]).then_inc(
                            sc.sems["sTcv"], 16)
            e.dma_start(dds[:, cc * C:(cc + 1) * C],
                        sds[:, cc * C:(cc + 1) * C]).then_inc(
                            sc.sems["sTcv"], 16)
            return None
        sc.bump("sTcv", 32)
        sc.emit("gpsimd", fc)
        tcnt[0] += 32
        tconv[kind][cc] = tcnt[0]

    for ch in range(n_prior + n_own):
        own = ch >= n_prior
        oc = ch - n_prior
        if own and ch == n_prior:
            if do_ffn:
                drain_quant(10 ** 9)
                qfence = sc.wall("sQw0", "sQw1", "sQl0", "sQl1", "sG", "sQg")
                for eng in ("sync", "gpsimd", "tensor", "vector", "scalar"):
                    sc.emit(eng, lambda e: None, waits=qfence)
            def fyp(e):
                e.dma_start(Y, Xc[n_prior * C:, :]).then_inc(
                    sc.sems["sYP"], 16)
                return None
            sc.bump("sYP", 16)
            sc.emit("gpsimd", fyp)
        ln_chunk(ch)
        if ch == 0:
            def fpc(e):
                e.dma_start(WkB, Wk).then_inc(sc.sems["sPC"], 16)
                e.dma_start(WvB, Wv).then_inc(sc.sems["sPCv"], 16)
                e.dma_start(WqB, Wq).then_inc(sc.sems["sPCq"], 16)
                return None
            sc.bump("sPC", 16)
            sc.bump("sPCv", 16)
            sc.bump("sPCq", 16)
            sc.emit("gpsimd", fpc)
        if ch == 0:
            units = [("K", 0)] + ([("K", 1)] if n_chunks > 1 else [])
            units += [("Q", 0)]
        elif ch == 1:
            units = [("Q", q) for q in range(1, n_own)]
            units += [("K", k) for k in range(2, n_chunks)]
        else:
            units = []
        for u in units:
            conv_unit(*u)
        if not own:
            drain_quant(2)
        if own:
            proj_rot(Qd3, WqB, TCQb, TSQb, tabq, oc * C, "sPCq",
                     tconv["Q"][oc])
        proj_rot(Kd3, WkB, TCKb, TSKb, tabk, ch * C, "sPC", tconv["K"][ch])
        if not own:
            drain_quant(3)
        proj_v()
        if not own:
            drain_quant(2)
        if own:
            scores()
            av_cross(oc)
        state_update(last=(own and ch == n_prior + n_own - 1))
        if not own:
            drain_quant(3)
        if own and do_ffn:
            ffn_chunk(oc)

    sc.emit("sync", lambda e: None,
            waits=[("sOut", sc.c("sOut")), ("sOut1", sc.c("sOut1"))])
    sc.build_block()
    return nc


_CACHE = {}


def kernel(L0_attn_X, L0_attn_W_Q, L0_attn_W_K, L0_attn_W_V,
           L0_ffn_w1, L0_ffn_w2, L0_ffn_w3):
    X = np.ascontiguousarray(np.asarray(L0_attn_X, dtype=np.float32))
    Wqh = np.ascontiguousarray(np.asarray(L0_attn_W_Q, dtype=np.float32)[:, PERM])
    Wkh = np.ascontiguousarray(np.asarray(L0_attn_W_K, dtype=np.float32)[:, PERM])
    Wvh = np.ascontiguousarray(np.asarray(L0_attn_W_V, dtype=np.float32))
    w1 = np.ascontiguousarray(np.asarray(L0_ffn_w1, dtype=np.float32))
    w2 = np.ascontiguousarray(np.asarray(L0_ffn_w2, dtype=np.float32))
    w3 = np.ascontiguousarray(np.asarray(L0_ffn_w3, dtype=np.float32))

    if "nc" not in _CACHE:
        _CACHE["nc"] = build_program(do_ffn=True, n_prior=NPRIOR, n_own=NCHUNK)
    nc = _CACHE["nc"]

    msk = host_masks()
    idn = np.eye(128, dtype=ml_dtypes.bfloat16)
    idg = (np.eye(128) * GC).astype(np.float32)

    in_maps = []
    for core in range(8):
        b = core // 4
        s = core % 4
        cq, sq, ck, sk = host_tables(s, NPRIOR, NCHUNK)
        own0 = s * SEG
        PT = NPRIOR * C
        if s == 0:
            xprior = np.zeros((PT, DM), dtype=np.float32)
        else:
            xprior = X[b, own0 - PT:own0]
        xctx = np.ascontiguousarray(
            np.concatenate([xprior, X[b, own0:own0 + SEG]], axis=0))
        in_maps.append(dict(Xc=xctx, Wq=Wqh, Wk=Wkh, Wv=Wvh,
                            TCQ=cq, TSQ=sq, TCK=ck, TSK=sk,
                            MSK=msk, IDN=idn, IDG=idg,
                            w1=w1, w2=w2, w3=w3))
    res = run_bass_kernel_spmd(nc, in_maps, core_ids=list(range(8)))
    out = np.empty((B, S, DM), dtype=res.results[0]["Y"].dtype)
    for core in range(8):
        b = core // 4
        s = core % 4
        out[b, s * SEG:(s + 1) * SEG] = res.results[core]["Y"]
    return out
